# version 01
# baseline (speedup 1.0000x reference)
import numpy as np
import ml_dtypes
import concourse.bass as bass
import concourse.mybir as mybir
from concourse.bass_utils import run_bass_kernel_spmd
from contextlib import ExitStack

F32 = mybir.dt.float32
BF16 = mybir.dt.bfloat16
FP16 = mybir.dt.float16
I32 = mybir.dt.int32
U8 = mybir.dt.uint8
AF = mybir.ActivationFunctionType
ALU = mybir.AluOpType
AX = mybir.AxisListType

NCORES = 8
P = 128
H = 2048
T = 2048
NHT = H // P
NTT = T // P
MT = 64
I_FFN = 8192
EPS = 1e-5
NEG_BIG = -1.0e30
GRP = 8


class B:
    def __init__(self, nc, stack):
        self.nc = nc
        self.stack = stack
        self.eng = {"sync": nc.sync, "vec": nc.vector, "act": nc.scalar,
                    "te": nc.tensor, "gp": nc.gpsimd}
        self.sem = {k: stack.enter_context(nc.semaphore(f"S_{k}"))
                    for k in self.eng}
        self.cnt = {k: 0 for k in self.eng}
        self.wm = {}

    def _chan(self, ch):
        if ch not in self.sem:
            self.sem[ch] = self.stack.enter_context(
                self.nc.semaphore(f"S_{ch}"))
            self.cnt[ch] = 0
        return ch

    def op(self, ename, fn, deps=(), dma=False, ch=None):
        e = self.eng[ename]
        flat = {}
        stack_ = list(deps)
        while stack_:
            tok = stack_.pop()
            if tok is None:
                continue
            if isinstance(tok, list):
                stack_.extend(tok)
                continue
            pname, pval = tok
            if pval > flat.get(pname, -1):
                flat[pname] = pval
        for pname, pval in flat.items():
            self._wait(e, ename, (pname, pval))
        inst = fn(e)
        if dma:
            assert ch is not None
            cname = self._chan(ch)
            inst.then_inc(self.sem[cname], 16)
            self.cnt[cname] += 16
            return (cname, self.cnt[cname])
        inst.then_inc(self.sem[ename], 1)
        self.cnt[ename] += 1
        return (ename, self.cnt[ename])

    def _wait(self, e, ename, tok):
        if tok is None:
            return
        peng, pval = tok
        if pval <= 0:
            return
        key = (ename, peng)
        if self.wm.get(key, -1) >= pval:
            return
        e.wait_ge(self.sem[peng], pval)
        self.wm[key] = pval

    def tok(self, name):
        return (name, self.cnt.get(name, 0))

    def barrier(self):
        snap = dict(self.cnt)
        for ename, e in self.eng.items():
            for pname, pval in snap.items():
                self._wait(e, ename, (pname, pval))

    def finish(self):
        snap = dict(self.cnt)
        e = self.eng["sync"]
        for pname, pval in snap.items():
            self._wait(e, "sync", (pname, pval))


def _softplus(b, src_tok, src_ap, a1, a2, a3, one_col, deps=()):
    t1 = b.op("act", lambda e: e.activation(a1, src_ap, AF.Abs),
              deps=[src_tok, *deps])
    t2 = b.op("act", lambda e: e.activation(a2, a1, AF.Exp, scale=-1.0),
              deps=[t1])
    t3 = b.op("act", lambda e: e.activation(a1, a2, AF.Ln, bias=one_col),
              deps=[t2])
    t4 = b.op("act", lambda e: e.activation(a3, src_ap, AF.Relu),
              deps=[src_tok, t3])
    t5 = b.op("vec", lambda e: e.tensor_add(a2, a1, a3), deps=[t3, t4])
    return t5, t4


def _glog(b, src_tok, src_ap, out_ap, a1, a2, a3, a4, a5, mskap,
          one_col, half_col, deps=()):
    tsp, trl = _softplus(b, src_tok, src_ap, a1, a2, a3, one_col, deps=deps)
    tb2 = b.op("vec", lambda e: e.tensor_sub(a4, src_ap, a2), deps=[tsp])
    tb1 = b.op("act", lambda e: e.activation(a5, a3, AF.Ln, bias=half_col),
               deps=[trl, tsp])
    tmk = b.op("vec", lambda e: e.tensor_scalar(mskap, src_ap, 0.0, None,
                                                ALU.is_ge), deps=[src_tok])
    tgl = b.op("vec", lambda e: e.select(out_ap, mskap, a5, a4, add_drain=True),
               deps=[tmk, tb1, tb2])
    return tgl


def build_l1():
    nc = bass.Bass("TRN2", target_bir_lowering=False)
    inp = lambda n, s, dt: nc.dram_tensor(n, s, dt, kind="ExternalInput").ap()
    outp = lambda n, s, dt: nc.dram_tensor(n, s, dt, kind="ExternalOutput").ap()

    x_t = inp("x_t", [T, H], F32)
    wz_blk = inp("wz_blk", [NHT, P, H], F32)
    wh_blk = inp("wh_blk", [NHT, P, H], F32)
    bz_c = inp("bz_c", [P, NHT], F32)
    bh_c = inp("bh_c", [P, NHT], F32)
    wg_blk = inp("wg_blk", [MT, P, H], I32)
    wu_blk = inp("wu_blk", [MT, P, H], I32)
    wd_t = inp("wd_t", [I_FFN, H], I32)
    sg_c = inp("sg_c", [P, MT], F32)
    su_c = inp("su_c", [P, MT], F32)
    tri = inp("tri", [P, P], F32)
    ones_row = inp("ones_row", [1, P], F32)
    ones_col = inp("ones_col", [P, 1], F32)
    identb = inp("identb", [P, P], BF16)

    xnT_o = outp("xnT_o", [H, T], BF16)
    w16_o = outp("w16_o", [H, T], FP16)
    L_o = outp("L_o", [P, NHT], F32)
    y_o = outp("y_o", [T, H], F32)

    with ExitStack() as st:
        b = B(nc, st)
        ccst = st.enter_context(nc.sbuf_tensor("ccst", [P, 4], F32))
        tcst = b.op("vec", lambda e: e.memset(ccst[:, 0:1], EPS))
        tcst = b.op("vec", lambda e: e.memset(ccst[:, 1:2], 1.0), deps=[tcst])
        tcst = b.op("vec", lambda e: e.memset(ccst[:, 2:3], 1.0), deps=[tcst])
        tcst = b.op("vec", lambda e: e.memset(ccst[:, 3:4], 0.5), deps=[tcst])
        Lb = st.enter_context(nc.sbuf_tensor("Lb", [P, NHT], F32))
        carA = st.enter_context(nc.sbuf_tensor("carA", [1, T], F32))
        carB = st.enter_context(nc.sbuf_tensor("carB", [1, T], F32))
        bzs = st.enter_context(nc.sbuf_tensor("bzs", [P, NHT], F32))
        bhs = st.enter_context(nc.sbuf_tensor("bhs", [P, NHT], F32))
        sgs = st.enter_context(nc.sbuf_tensor("sgs", [P, MT], F32))
        sus = st.enter_context(nc.sbuf_tensor("sus", [P, MT], F32))
        tris = st.enter_context(nc.sbuf_tensor("tris", [P, P], F32))
        onrs = st.enter_context(nc.sbuf_tensor("onrs", [1, P], F32))
        oncs = st.enter_context(nc.sbuf_tensor("oncs", [P, 1], F32))
        idbs = st.enter_context(nc.sbuf_tensor("idbs", [P, P], BF16))

        tc = None
        for dst, src in ((bzs, bz_c), (bhs, bh_c), (sgs, sg_c), (sus, su_c),
                         (tris, tri), (onrs, ones_row), (oncs, ones_col),
                         (idbs, identb)):
            tc = b.op("sync", lambda e, dd=dst, ss=src:
                      e.dma_start(out=dd[:], in_=ss), dma=True, ch="cst")
        t_consts = tc
        tcar0 = b.op("vec", lambda e: e.memset(carA[:], 0.0))

        with ExitStack() as s0:
            xnT = s0.enter_context(nc.sbuf_tensor("xnT", [P, NHT * T], BF16))
            with ExitStack() as sp0:
                xb = [sp0.enter_context(nc.sbuf_tensor(f"xb{i}", [P, H], F32))
                      for i in range(2)]
                xc = sp0.enter_context(nc.sbuf_tensor("xc", [P, H], F32))
                sqd = sp0.enter_context(nc.sbuf_tensor("sqd", [P, H], F32))
                xn16 = [sp0.enter_context(
                    nc.sbuf_tensor(f"xn16{i}", [P, H], BF16)) for i in range(2)]
                cls = sp0.enter_context(nc.sbuf_tensor("cls", [P, 8], F32))
                pq = [sp0.enter_context(
                    nc.psum_tensor(f"pq{i}", [P, 1024], BF16))
                    for i in range(2)]

                tx = [None, None]
                txn = [None, None]
                tev = {0: None, 1: None}
                tscr_prev = None
                t4_prev = None
                for rt in range(NTT):
                    pi = rt % 2
                    tx[pi] = b.op("sync", lambda e, rr=rt, p=pi: e.dma_start(
                        out=xb[p][:], in_=x_t[rr * P:(rr + 1) * P, :]),
                        deps=[txn[pi]], dma=True, ch=f"x{pi}")
                    t1 = b.op("vec", lambda e, p=pi: e.reduce_sum(
                        cls[:, 0:1], xb[p][:], AX.X, negate=True),
                        deps=[tx[pi], tscr_prev])
                    t2 = b.op("vec", lambda e: e.tensor_scalar(
                        cls[:, 1:2], cls[:, 0:1], 1.0 / H, None, ALU.mult),
                        deps=[t1])
                    t3 = b.op("vec", lambda e, p=pi: e.tensor_scalar(
                        xc[:], xb[p][:], cls[:, 1:2], None, ALU.add),
                        deps=[t2, t4_prev])
                    t4 = b.op("act", lambda e: e.activation(
                        sqd[:], xc[:], AF.Square, accum_out=cls[:, 2:3]),
                        deps=[t3])
                    t4_prev = t4
                    t5 = b.op("vec", lambda e: e.tensor_scalar(
                        cls[:, 3:4], cls[:, 2:3], 1.0 / H, None, ALU.mult),
                        deps=[t4])
                    t6 = b.op("act", lambda e: e.activation(
                        cls[:, 4:5], cls[:, 3:4], AF.Ln, bias=ccst[:, 0:1]), deps=[t5])
                    t7 = b.op("act", lambda e: e.activation(
                        cls[:, 5:6], cls[:, 4:5], AF.Exp, scale=-0.5),
                        deps=[t6])
                    txn[pi] = b.op("vec", lambda e, p=pi: e.tensor_scalar(
                        xn16[p][:], xc[:], cls[:, 5:6], None, ALU.mult),
                        deps=[t7, t3])
                    tscr_prev = txn[pi]
                    for q in range(4):
                        qp = q % 2
                        def tpg(e, p=pi, qq=q, qpp=qp):
                            last = None
                            for j in range(4):
                                hi = qq * 4 + j
                                last = e.transpose(
                                    pq[qpp][:, j * P:(j + 1) * P],
                                    xn16[p][:, hi * P:(hi + 1) * P], idbs[:])
                            return last
                        ttp = b.op("te", tpg,
                                   deps=[txn[pi], t_consts, tev[qp]])
                        def ev(e, rr=rt, qq=q, qpp=qp):
                            dst = xnT[:].rearrange(
                                "p (k t) -> p k t", k=NHT)[
                                :, qq * 4:(qq + 1) * 4, rr * P:(rr + 1) * P]
                            return e.tensor_copy(
                                dst, pq[qpp][:, 0:512])
                        tev[qp] = b.op("vec", ev, deps=[ttp])
                t_xnT = [tev[0], tev[1]]
                for ki in range(NHT):
                    b.op("sync", lambda e, k=ki: e.dma_start(
                        out=xnT_o[k * P:(k + 1) * P, :],
                        in_=xnT[:, k * T:(k + 1) * T]),
                        deps=t_xnT, dma=True, ch="xe")
            b.barrier()

            with ExitStack() as s1:
                wzs = [s1.enter_context(
                    nc.sbuf_tensor(f"wzs{i}", [P, H], BF16)) for i in range(2)]
                whs = [s1.enter_context(
                    nc.sbuf_tensor(f"whs{i}", [P, H], BF16)) for i in range(2)]
                kbuf = s1.enter_context(nc.sbuf_tensor("kbuf", [P, T], F32))
                khb = s1.enter_context(nc.sbuf_tensor("khb", [P, T], F32))
                a1 = s1.enter_context(nc.sbuf_tensor("a1", [P, T], F32))
                a2 = s1.enter_context(nc.sbuf_tensor("a2", [P, T], F32))
                a3 = s1.enter_context(nc.sbuf_tensor("a3", [P, T], F32))
                a4 = s1.enter_context(nc.sbuf_tensor("a4", [P, T], F32))
                a5 = s1.enter_context(nc.sbuf_tensor("a5", [P, T], F32))
                spb = s1.enter_context(nc.sbuf_tensor("spb", [P, T], F32))
                gbuf = s1.enter_context(nc.sbuf_tensor("gbuf", [P, T], F32))
                msk = s1.enter_context(nc.sbuf_tensor("msk", [P, T], U8))
                vtl = s1.enter_context(nc.sbuf_tensor("vtl", [P, T], F32))
                w16t = s1.enter_context(nc.sbuf_tensor("w16t", [P, T], FP16))
                edmp = s1.enter_context(nc.sbuf_tensor("edmp", [P, T], BF16))
                clp = s1.enter_context(nc.sbuf_tensor("clp", [P, 4], F32))
                psA = s1.enter_context(nc.psum_tensor("psA", [P, T], F32))
                psB = s1.enter_context(nc.psum_tensor("psB", [P, T], F32))

                twz = [None, None]
                twh = [None, None]
                twzu = [None, None]
                twhu = [None, None]
                twz[0] = b.op("gp", lambda e: e.dma_start(
                    out=wzs[0][:], in_=wz_blk[0]), dma=True, ch="wz0")
                twh[0] = b.op("gp", lambda e: e.dma_start(
                    out=whs[0][:], in_=wh_blk[0]), dma=True, ch="wh0")

                cars = [carA, carB]
                tcar = [tcar0, None]
                tA_used = None
                tB_used = None
                tj = None
                tw16dma = None
                tLast = None
                for hi in range(NHT):
                    pi = hi % 2
                    if hi + 1 < NHT:
                        twz[1 - pi] = b.op("gp", lambda e, h=hi: e.dma_start(
                            out=wzs[1 - (h % 2)][:], in_=wz_blk[h + 1]),
                            deps=[twzu[1 - pi]], dma=True, ch=f"wz{1 - pi}")
                        twh[1 - pi] = b.op("gp", lambda e, h=hi: e.dma_start(
                            out=whs[1 - (h % 2)][:], in_=wh_blk[h + 1]),
                            deps=[twhu[1 - pi]], dma=True, ch=f"wh{1 - pi}")

                    def kmm(e, p=pi):
                        last = None
                        for ki in range(NHT):
                            for ns in range(4):
                                last = e.matmul(
                                    psA[:, ns * 512:(ns + 1) * 512],
                                    wzs[p][:, ki * P:(ki + 1) * P],
                                    xnT[:, ki * T + ns * 512:
                                        ki * T + (ns + 1) * 512],
                                    start=(ki == 0), stop=(ki == NHT - 1))
                        return last
                    tkmm = b.op("te", kmm, deps=[twz[pi], t_xnT, tA_used])
                    twzu[pi] = tkmm
                    tkb = b.op("vec", lambda e, h=hi: e.tensor_scalar(
                        kbuf[:], psA[:], bzs[:, h:h + 1], None, ALU.add),
                        deps=[tkmm, t_consts, tj])

                    def khmm(e, p=pi):
                        last = None
                        for ki in range(NHT):
                            for ns in range(4):
                                last = e.matmul(
                                    psB[:, ns * 512:(ns + 1) * 512],
                                    whs[p][:, ki * P:(ki + 1) * P],
                                    xnT[:, ki * T + ns * 512:
                                        ki * T + (ns + 1) * 512],
                                    start=(ki == 0), stop=(ki == NHT - 1))
                        return last
                    tkhmm = b.op("te", khmm, deps=[twh[pi], t_xnT, tB_used])
                    twhu[pi] = tkhmm
                    tkhb = b.op("vec", lambda e, h=hi: e.tensor_scalar(
                        khb[:], psB[:], bhs[:, h:h + 1], None, ALU.add),
                        deps=[tkhmm, tj])

                    tsp, _ = _softplus(b, tkb, kbuf[:], a1[:], a2[:], a3[:],
                                       ccst[:, 2:3], deps=[tj])
                    tspc = b.op("vec", lambda e: e.tensor_copy(spb[:], a2[:]),
                                deps=[tsp])
                    tgl = _glog(b, tkhb, khb[:], gbuf[:], a1[:], a2[:], a3[:],
                                a4[:], a5[:], msk[:], ccst[:, 2:3],
                                ccst[:, 3:4], deps=[tspc])

                    car_p = cars[hi % 2]
                    car_n = cars[1 - hi % 2]

                    def smm(e, cp=car_p):
                        last = None
                        for ns in range(4):
                            e.matmul(psA[:, ns * 512:(ns + 1) * 512], tris[:],
                                     spb[:, ns * 512:(ns + 1) * 512],
                                     start=True, stop=False)
                            last = e.matmul(
                                psA[:, ns * 512:(ns + 1) * 512], onrs[:],
                                cp[:, ns * 512:(ns + 1) * 512],
                                start=False, stop=True)
                        return last
                    tS = b.op("te", smm,
                              deps=[tspc, tcar[hi % 2], tkb, t_consts])

                    def cmm(e):
                        last = None
                        for ns in range(4):
                            last = e.matmul(
                                psB[0:1, ns * 512:(ns + 1) * 512], oncs[:],
                                spb[:, ns * 512:(ns + 1) * 512],
                                start=True, stop=True)
                        return last
                    tcs = b.op("te", cmm, deps=[tspc, tkhb, tgl])
                    tcar[1 - hi % 2] = b.op(
                        "vec", lambda e, cn=car_n, cp=car_p: e.tensor_add(
                            cn[:], cp[:], psB[0:1, :]),
                        deps=[tcs, tcar[1 - hi % 2], tcar[hi % 2]])
                    tB_used = tcar[1 - hi % 2]

                    tw1 = b.op("vec", lambda e: e.tensor_sub(
                        a1[:], kbuf[:], spb[:]), deps=[tspc, tkb, tgl])
                    tw = b.op("vec", lambda e: e.tensor_add(
                        a4[:], a1[:], gbuf[:]), deps=[tw1, tgl])
                    tw16 = b.op("vec", lambda e: e.tensor_copy(
                        w16t[:], a4[:]), deps=[tw, tw16dma])
                    tw16dma = b.op("sync", lambda e, h=hi: e.dma_start(
                        out=w16_o[h * P:(h + 1) * P, :], in_=w16t[:]),
                        deps=[tw16], dma=True, ch="w16")
                    tv = b.op("vec", lambda e: e.tensor_add(
                        vtl[:], a4[:], psA[:]), deps=[tw, tS])
                    tA_used = tv
                    tnm = b.op("vec", lambda e: e.reduce_max(
                        clp[:, 0:1], vtl[:], AX.X, negate=True),
                        deps=[tv, tj])
                    te_ = b.op("act", lambda e: e.activation(
                        edmp[:], vtl[:], AF.Exp, bias=clp[:, 0:1],
                        accum_out=clp[:, 1:2]), deps=[tnm])
                    tls = b.op("act", lambda e: e.activation(
                        clp[:, 2:3], clp[:, 1:2], AF.Ln), deps=[te_])
                    tLast = b.op("vec", lambda e, h=hi: e.tensor_sub(
                        Lb[:, h:h + 1], clp[:, 2:3], clp[:, 0:1]), deps=[tls])
                    tj = [tLast, te_, tw16, tgl]
                b.op("sync", lambda e: e.dma_start(out=L_o, in_=Lb[:]),
                     deps=[tLast], dma=True, ch="L")
            b.barrier()

        b.barrier()
        with ExitStack() as s2:
            xnTh = s2.enter_context(
                nc.sbuf_tensor("xnTh", [P, NHT * 1024], BF16))
            ysb = s2.enter_context(nc.sbuf_tensor("ysb", [P, 8 * H], F32))
            wdc = [s2.enter_context(
                nc.sbuf_tensor(f"wdc{i}", [P, H], BF16)) for i in range(GRP)]
            hcc = [s2.enter_context(
                nc.sbuf_tensor(f"hc{i}", [P, 1024], BF16)) for i in range(GRP)]
            wgb = [s2.enter_context(
                nc.sbuf_tensor(f"wgb{i}", [P, H], BF16)) for i in range(2)]
            wub = [s2.enter_context(
                nc.sbuf_tensor(f"wub{i}", [P, H], BF16)) for i in range(2)]
            sil = s2.enter_context(nc.sbuf_tensor("sil", [P, 1024], F32))
            psG = s2.enter_context(nc.psum_tensor("psG", [P, 1024], F32))
            psU = s2.enter_context(nc.psum_tensor("psU", [P, 1024], F32))
            psY = [s2.enter_context(
                nc.psum_tensor(f"psY{i}", [P, 512], F32)) for i in range(2)]

            for th in range(2):
                if th == 1:
                    b.barrier()
                txh = []
                for ki in range(NHT):
                    txh.append(b.op("sync", lambda e, k=ki, tt=th: e.dma_start(
                        out=xnTh[:, k * 1024:(k + 1) * 1024],
                        in_=xnT_o[k * P:(k + 1) * P,
                                  tt * 1024:(tt + 1) * 1024]),
                        deps=[b.tok("xe")], dma=True, ch="xh"))
                tyz = b.op("vec", lambda e: e.memset(ysb[:], 0.0))
                tgmm_l = [None, None]
                tumm_l = [None, None]
                tcg = [None, None]
                tcu = [None, None]
                tcd = [None] * GRP
                tdown_last = None
                th_tok = [None] * GRP
                tsil_u = None
                ty_ev = {}
                for mt in range(MT):
                    pi = mt % 2
                    slot = mt % GRP
                    tm1 = b.op("vec", lambda e, p=pi: e.memset(
                        wgb[p][:], -128.0), deps=[tgmm_l[pi]])
                    tcg[pi] = b.op("gp", lambda e, p=pi, m=mt: e.dma_start(
                        out=wgb[p][:], in_=wg_blk[m], accum_op=ALU.add),
                        deps=[tm1], dma=True, ch=f"wg{pi}")
                    tm2 = b.op("vec", lambda e, p=pi: e.memset(
                        wub[p][:], -128.0), deps=[tumm_l[pi]])
                    tcu[pi] = b.op("gp", lambda e, p=pi, m=mt: e.dma_start(
                        out=wub[p][:], in_=wu_blk[m], accum_op=ALU.add),
                        deps=[tm2], dma=True, ch=f"wu{pi}")
                    tm3 = b.op("vec", lambda e, s=slot: e.memset(
                        wdc[s][:], -128.0), deps=[tdown_last])
                    tcd[slot] = b.op("gp", lambda e, s=slot, m=mt: e.dma_start(
                        out=wdc[s][:], in_=wd_t[m * P:(m + 1) * P, :],
                        accum_op=ALU.add), deps=[tm3], dma=True,
                        ch=f"wd{slot}")

                    def gmm(e, p=pi):
                        last = None
                        for ki in range(NHT):
                            for ns in range(2):
                                last = e.matmul(
                                    psG[:, ns * 512:(ns + 1) * 512],
                                    wgb[p][:, ki * P:(ki + 1) * P],
                                    xnTh[:, ki * 1024 + ns * 512:
                                         ki * 1024 + (ns + 1) * 512],
                                    start=(ki == 0), stop=(ki == NHT - 1))
                        return last
                    tg = b.op("te", gmm, deps=[tcg[pi], txh, tgmm_l[pi]])
                    tgmm_l[pi] = tg

                    def umm(e, p=pi):
                        last = None
                        for ki in range(NHT):
                            for ns in range(2):
                                last = e.matmul(
                                    psU[:, ns * 512:(ns + 1) * 512],
                                    wub[p][:, ki * P:(ki + 1) * P],
                                    xnTh[:, ki * 1024 + ns * 512:
                                         ki * 1024 + (ns + 1) * 512],
                                    start=(ki == 0), stop=(ki == NHT - 1))
                        return last
                    tu = b.op("te", umm, deps=[tcu[pi], txh, tumm_l[pi]])
                    tumm_l[pi] = tu

                    tsl = b.op("act", lambda e, m=mt: e.activation(
                        sil[:], psG[:], AF.Silu, scale=sgs[:, m:m + 1]),
                        deps=[tg, t_consts, tsil_u])
                    tgmm_l[pi] = [tgmm_l[pi], tsl]
                    tH = b.op("vec", lambda e, s=slot, m=mt:
                              e.scalar_tensor_tensor(
                                  hcc[s][:], psU[:], sus[:, m:m + 1], sil[:],
                                  ALU.mult, ALU.mult),
                              deps=[tu, tsl, th_tok[slot]])
                    tsil_u = tH
                    tumm_l[pi] = [tumm_l[pi], tH]
                    th_tok[slot] = tH

                    if slot == GRP - 1:
                        tprev_ev = None
                        tdl = None
                        for tb in range(8):
                            for osl in range(4):
                                yi = (tb * 4 + osl) % 2
                                def dmm(e, tbb=tb, oo=osl, y=yi):
                                    last = None
                                    for j in range(GRP):
                                        last = e.matmul(
                                            psY[y][:],
                                            hcc[j][:, tbb * P:(tbb + 1) * P],
                                            wdc[j][:, oo * 512:(oo + 1) * 512],
                                            start=(j == 0),
                                            stop=(j == GRP - 1))
                                    return last
                                tdm = b.op("te", dmm,
                                           deps=[th_tok, tcd,
                                                 ty_ev.get((tb, osl) if False
                                                           else yi)])
                                def yev(e, tbb=tb, oo=osl, y=yi):
                                    dst = ysb[:, tbb * H + oo * 512:
                                              tbb * H + (oo + 1) * 512]
                                    return e.tensor_add(dst, dst, psY[y][:])
                                tev_ = b.op("vec", yev, deps=[tdm, tyz])
                                ty_ev[yi] = tev_
                                tdl = tdm
                        tdown_last = [tdl, ty_ev.get(0), ty_ev.get(1)]
                        th_tok = [tdown_last] * GRP
                tyd = []
                for tb in range(8):
                    tyd.append(b.op("sync", lambda e, tt=th, tbb=tb:
                                    e.dma_start(
                        out=y_o[tt * 1024 + tbb * P:tt * 1024 + (tbb + 1) * P,
                                :],
                        in_=ysb[:, tbb * H:(tbb + 1) * H]),
                        deps=[ty_ev.get(0), ty_ev.get(1)], dma=True,
                        ch="y"))
        b.finish()
    return nc


def build_l2():
    nc = bass.Bass("TRN2", target_bir_lowering=False)
    inp = lambda n, s, dt: nc.dram_tensor(n, s, dt, kind="ExternalInput").ap()
    outp = lambda n, s, dt: nc.dram_tensor(n, s, dt, kind="ExternalOutput").ap()

    xnT_i = inp("xnT_i", [H, T], BF16)
    w16_i = inp("w16_i", [H, T], FP16)
    G_pm = inp("G_pm", [NCORES, P, NHT], F32)
    h0_pm = inp("h0_pm", [P, NHT], F32)
    masks = inp("masks", [P, 144], F32)
    wz_blk = inp("wz_blk", [NHT, P, H], F32)
    bz_c = inp("bz_c", [P, NHT], F32)
    tri = inp("tri", [P, P], F32)
    ones_row = inp("ones_row", [1, P], F32)
    ones_col = inp("ones_col", [P, 1], F32)
    wc_in = inp("wc_in", [1, 1], F32)

    logh_o = outp("logh_o", [H, T], F32)
    stats_o = outp("stats_o", [1, 4], F32)

    cdram = nc.dram_tensor("cdram", [1, H], F32)

    with ExitStack() as st:
        b = B(nc, st)
        ccst = st.enter_context(nc.sbuf_tensor("ccst", [P, 4], F32))
        tcst = b.op("vec", lambda e: e.memset(ccst[:, 0:1], EPS))
        tcst = b.op("vec", lambda e: e.memset(ccst[:, 2:3], 1.0), deps=[tcst])
        tcst = b.op("vec", lambda e: e.memset(ccst[:, 3:4], 0.5), deps=[tcst])
        xnT = st.enter_context(nc.sbuf_tensor("xnT", [P, NHT * T], BF16))
        bzs = st.enter_context(nc.sbuf_tensor("bzs", [P, NHT], F32))
        tris = st.enter_context(nc.sbuf_tensor("tris", [P, P], F32))
        onrs = st.enter_context(nc.sbuf_tensor("onrs", [1, P], F32))
        oncs = st.enter_context(nc.sbuf_tensor("oncs", [P, 1], F32))
        carA = st.enter_context(nc.sbuf_tensor("carA", [1, T], F32))
        carB = st.enter_context(nc.sbuf_tensor("carB", [1, T], F32))
        ccols = st.enter_context(nc.sbuf_tensor("ccols", [P, NHT], F32))
        statS = st.enter_context(nc.sbuf_tensor("statS", [P, NHT], F32))
        statQ = st.enter_context(nc.sbuf_tensor("statQ", [P, NHT], F32))
        glg0 = st.enter_context(nc.sbuf_tensor("glg0", [P, NHT], F32))
        h0s = st.enter_context(nc.sbuf_tensor("h0s", [P, NHT], F32))
        cmb = st.enter_context(nc.sbuf_tensor("cmb", [P, 144], F32))
        cmb2 = st.enter_context(nc.sbuf_tensor("cmb2", [P, 144], F32))
        msks = st.enter_context(nc.sbuf_tensor("msks", [P, 144], F32))
        s16a = st.enter_context(nc.sbuf_tensor("s16a", [P, NHT], F32))
        s16b = st.enter_context(nc.sbuf_tensor("s16b", [P, NHT], F32))
        s16c = st.enter_context(nc.sbuf_tensor("s16c", [P, NHT], F32))
        s16d = st.enter_context(nc.sbuf_tensor("s16d", [P, NHT], F32))
        s16e = st.enter_context(nc.sbuf_tensor("s16e", [P, NHT], F32))
        mk16 = st.enter_context(nc.sbuf_tensor("mk16", [P, NHT], U8))
        wcs = st.enter_context(nc.sbuf_tensor("wcs", [1, 1], F32))
        fin = st.enter_context(nc.sbuf_tensor("fin", [1, 16], F32))

        tc = None
        for dst, src in ((bzs, bz_c), (tris, tri), (onrs, ones_row),
                         (oncs, ones_col), (h0s, h0_pm), (msks, masks),
                         (wcs, wc_in)):
            tc = b.op("sync", lambda e, dd=dst, ss=src: e.dma_start(
                out=dd[:], in_=ss), dma=True, ch="cst")
        t_consts = tc
        txnT = []
        for ki in range(NHT):
            txnT.append(b.op("sync", lambda e, k=ki: e.dma_start(
                out=xnT[:, k * T:(k + 1) * T],
                in_=xnT_i[k * P:(k + 1) * P, :]), dma=True, ch="xn"))
        tcar0 = b.op("vec", lambda e: e.memset(carA[:], 0.0))

        tg0 = _glog(b, t_consts, h0s[:], glg0[:], s16a[:], s16b[:], s16c[:],
                    s16d[:], s16e[:], mk16[:], ccst[:, 2:3], ccst[:, 3:4],
                    deps=[t_consts, tcst])

        tgl = []
        for j in range(NCORES):
            tgl.append(b.op("sync", lambda e, jj=j: e.dma_start(
                out=cmb[:, jj * NHT:(jj + 1) * NHT], in_=G_pm[jj]),
                dma=True, ch="G"))
        tcp = b.op("vec", lambda e: e.tensor_copy(
            cmb[:, NCORES * NHT:(NCORES + 1) * NHT], glg0[:]),
            deps=[tg0, *tgl])
        tma = b.op("vec", lambda e: e.tensor_add(cmb2[:], cmb[:], msks[:]),
                   deps=[tcp, t_consts])
        cmb3 = cmb2[:].rearrange("p (j t) -> p t j", j=9)
        tmx = b.op("vec", lambda e: e.reduce_max(
            s16a[:], cmb3, AX.X, negate=True), deps=[tma])
        tsh = None
        for j in range(9):
            tsh = b.op("vec", lambda e, jj=j: e.scalar_tensor_tensor(
                cmb[:, jj * NHT:(jj + 1) * NHT],
                cmb2[:, jj * NHT:(jj + 1) * NHT], 1.0, s16a[:],
                ALU.mult, ALU.add), deps=[tmx, tma, tsh])
        tex = b.op("act", lambda e: e.activation(cmb2[:], cmb[:], AF.Exp),
                   deps=[tsh])
        cmb4 = cmb2[:].rearrange("p (j t) -> p t j", j=9)
        tsm = b.op("vec", lambda e: e.reduce_sum(s16b[:], cmb4, AX.X),
                   deps=[tex])
        tln = b.op("act", lambda e: e.activation(s16c[:], s16b[:], AF.Ln),
                   deps=[tsm])
        tcar_pm = b.op("vec", lambda e: e.tensor_sub(s16d[:], s16c[:],
                                                     s16a[:]), deps=[tln])
        tcd = b.op("sync", lambda e: e.dma_start(
            out=cdram.ap().rearrange("one (p c) -> (one p) c", p=P),
            in_=s16d[:]), deps=[tcar_pm], dma=True, ch="cw")
        tcc = []
        for hi in range(NHT):
            tcc.append(b.op("sync", lambda e, h=hi: e.dma_start(
                out=ccols[:, h:h + 1],
                in_=cdram.ap().rearrange(
                    "one (c p) -> (one c) p", c=NHT)[h:h + 1, :].rearrange(
                    "c p -> p c")), deps=[tcd], dma=True, ch="cr"))

        with ExitStack() as s1:
            wzs = [s1.enter_context(
                nc.sbuf_tensor(f"wzs{i}", [P, H], BF16)) for i in range(2)]
            w16s = [s1.enter_context(
                nc.sbuf_tensor(f"w16s{i}", [P, T], FP16)) for i in range(2)]
            kbuf = s1.enter_context(nc.sbuf_tensor("kbuf", [P, T], F32))
            a1 = s1.enter_context(nc.sbuf_tensor("a1", [P, T], F32))
            a2 = s1.enter_context(nc.sbuf_tensor("a2", [P, T], F32))
            a3 = s1.enter_context(nc.sbuf_tensor("a3", [P, T], F32))
            spb = s1.enter_context(nc.sbuf_tensor("spb", [P, T], F32))
            vb = s1.enter_context(nc.sbuf_tensor("vb", [P, T], F32))
            mb = s1.enter_context(nc.sbuf_tensor("mb", [P, T + 8], F32))
            lh = s1.enter_context(nc.sbuf_tensor("lh", [P, T], F32))
            sqd = s1.enter_context(nc.sbuf_tensor("sqd", [P, T], F32))
            psA = s1.enter_context(nc.psum_tensor("psA", [P, T], F32))
            psB = s1.enter_context(nc.psum_tensor("psB", [P, T], F32))

            twz = [None, None]
            tww = [None, None]
            twzu = [None, None]
            twwu = [None, None]
            twz[0] = b.op("gp", lambda e: e.dma_start(
                out=wzs[0][:], in_=wz_blk[0]), dma=True, ch="wz0")
            tww[0] = b.op("sync", lambda e: e.dma_start(
                out=w16s[0][:], in_=w16_i[0:P, :]), dma=True, ch="w0")
            cars = [carA, carB]
            tcar = [tcar0, None]
            tA_used = None
            tB_used = None
            tj = None
            tlh_dma = None
            tstat_last = None
            for hi in range(NHT):
                pi = hi % 2
                if hi + 1 < NHT:
                    twz[1 - pi] = b.op("gp", lambda e, h=hi: e.dma_start(
                        out=wzs[1 - (h % 2)][:], in_=wz_blk[h + 1]),
                        deps=[twzu[1 - pi]], dma=True, ch=f"wz{1 - pi}")
                    tww[1 - pi] = b.op("sync", lambda e, h=hi: e.dma_start(
                        out=w16s[1 - (h % 2)][:],
                        in_=w16_i[(h + 1) * P:(h + 2) * P, :]),
                        deps=[twwu[1 - pi]], dma=True, ch=f"w{1 - pi}")

                def kmm(e, p=pi):
                    last = None
                    for ki in range(NHT):
                        for ns in range(4):
                            last = e.matmul(
                                psA[:, ns * 512:(ns + 1) * 512],
                                wzs[p][:, ki * P:(ki + 1) * P],
                                xnT[:, ki * T + ns * 512:
                                    ki * T + (ns + 1) * 512],
                                start=(ki == 0), stop=(ki == NHT - 1))
                    return last
                tkmm = b.op("te", kmm, deps=[twz[pi], txnT, tA_used])
                twzu[pi] = tkmm
                tkb = b.op("vec", lambda e, h=hi: e.tensor_scalar(
                    kbuf[:], psA[:], bzs[:, h:h + 1], None, ALU.add),
                    deps=[tkmm, t_consts, tj])
                tsp, _ = _softplus(b, tkb, kbuf[:], a1[:], a2[:], a3[:],
                                   ccst[:, 2:3], deps=[tj])
                tspc = b.op("vec", lambda e: e.tensor_copy(spb[:], a2[:]),
                            deps=[tsp])
                car_p = cars[hi % 2]
                car_n = cars[1 - hi % 2]

                def smm(e, cp=car_p):
                    last = None
                    for ns in range(4):
                        e.matmul(psA[:, ns * 512:(ns + 1) * 512], tris[:],
                                 spb[:, ns * 512:(ns + 1) * 512],
                                 start=True, stop=False)
                        last = e.matmul(psA[:, ns * 512:(ns + 1) * 512],
                                        onrs[:], cp[:, ns * 512:(ns + 1) * 512],
                                        start=False, stop=True)
                    return last
                tS = b.op("te", smm, deps=[tspc, tcar[hi % 2], tkb, t_consts])

                def cmm(e):
                    last = None
                    for ns in range(4):
                        last = e.matmul(psB[0:1, ns * 512:(ns + 1) * 512],
                                        oncs[:],
                                        spb[:, ns * 512:(ns + 1) * 512],
                                        start=True, stop=True)
                    return last
                tcs = b.op("te", cmm, deps=[tspc, tB_used])
                tcar[1 - hi % 2] = b.op(
                    "vec", lambda e, cn=car_n, cp=car_p: e.tensor_add(
                        cn[:], cp[:], psB[0:1, :]),
                    deps=[tcs, tcar[1 - hi % 2], tcar[hi % 2]])
                tB_used = tcar[1 - hi % 2]

                tv = b.op("vec", lambda e, p=pi: e.tensor_add(
                    vb[:], w16s[p][:], psA[:]), deps=[tww[pi], tS, tj])
                twwu[pi] = tv
                tm0 = b.op("vec", lambda e, h=hi: e.tensor_copy(
                    mb[:, 0:1], ccols[:, h:h + 1]), deps=[tcc[-1], tj])
                tsc1 = b.op("vec", lambda e, h=hi: e.tensor_tensor_scan(
                    mb[:, 1:T + 1], vb[:], vb[:], ccols[:, h:h + 1],
                    ALU.max, ALU.bypass), deps=[tv, tm0])
                td1 = b.op("vec", lambda e: e.tensor_sub(
                    a1[:], mb[:, 0:T], mb[:, 1:T + 1]), deps=[tsc1])
                ta_ = b.op("act", lambda e: e.activation(a2[:], a1[:], AF.Exp),
                           deps=[td1])
                td2 = b.op("vec", lambda e: e.tensor_sub(
                    a3[:], vb[:], mb[:, 1:T + 1]), deps=[tsc1, tv])
                tb_ = b.op("act", lambda e: e.activation(kbuf[:], a3[:],
                                                         AF.Exp), deps=[td2])
                tsc2 = b.op("vec", lambda e: e.tensor_tensor_scan(
                    a3[:], a2[:], kbuf[:], 1.0, ALU.mult, ALU.add),
                    deps=[ta_, tb_])
                tlnc = b.op("act", lambda e: e.activation(a1[:], a3[:], AF.Ln),
                            deps=[tsc2])
                tt1 = b.op("vec", lambda e: e.tensor_sub(
                    a2[:], mb[:, 1:T + 1], psA[:]), deps=[tlnc, tS, tsc1])
                tA_used = tt1
                tlh = b.op("vec", lambda e: e.tensor_add(lh[:], a2[:], a1[:]),
                           deps=[tt1, tlnc, tlh_dma])
                tss = b.op("vec", lambda e, h=hi: e.reduce_sum(
                    statS[:, h:h + 1], lh[:], AX.X), deps=[tlh])
                tsq = b.op("act", lambda e, h=hi: e.activation(
                    sqd[:], lh[:], AF.Square, accum_out=statQ[:, h:h + 1]),
                    deps=[tlh])
                tlh_dma = b.op("sync", lambda e, h=hi: e.dma_start(
                    out=logh_o[h * P:(h + 1) * P, :], in_=lh[:]),
                    deps=[tlh], dma=True, ch="lh")
                tstat_last = [tss, tsq]
                tj = [tss, tsq, tsc2, tb_]

            ts1 = b.op("vec", lambda e: e.tensor_mul(s16a[:], statS[:],
                                                     statS[:]),
                       deps=[tstat_last, tg0])
            ts2 = b.op("vec", lambda e: e.tensor_scalar(
                s16b[:], s16a[:], 1.0 / T, None, ALU.mult), deps=[ts1])
            ts3 = b.op("vec", lambda e: e.tensor_sub(s16c[:], statQ[:],
                                                     s16b[:]), deps=[ts2])
            tr1 = b.op("vec", lambda e: e.reduce_sum(s16d[:, 0:1], s16c[:],
                                                     AX.X), deps=[ts3])
            tr2 = b.op("vec", lambda e: e.reduce_sum(s16d[:, 1:2], s16b[:],
                                                     AX.X), deps=[ts3])
            tr3 = b.op("vec", lambda e: e.reduce_sum(s16d[:, 2:3], statS[:],
                                                     AX.X), deps=[ts1])
            tq0 = b.op("act", lambda e: e.activation(
                s16e[:], glg0[:], AF.Square, accum_out=s16d[:, 3:4]),
                deps=[tg0])
            tr4 = b.op("vec", lambda e: e.reduce_sum(s16d[:, 4:5], glg0[:],
                                                     AX.X), deps=[tg0, tq0])
            tpm = b.op("te", lambda e: e.matmul(
                psB[0:1, 0:8], oncs[:], s16d[:, 0:8], start=True, stop=True),
                deps=[tr1, tr2, tr3, tq0, tr4, t_consts, tB_used])
            tf = b.op("vec", lambda e: e.tensor_copy(fin[0:1, 0:8],
                                                     psB[0:1, 0:8]),
                      deps=[tpm])
            NL = float(NHT * P * T)
            t_a = b.op("vec", lambda e: e.tensor_mul(
                fin[0:1, 8:9], fin[0:1, 2:3], fin[0:1, 2:3]), deps=[tf])
            t_b = b.op("vec", lambda e: e.tensor_scalar(
                fin[0:1, 9:10], fin[0:1, 8:9], 1.0 / NL, None, ALU.mult),
                deps=[t_a])
            t_c = b.op("vec", lambda e: e.tensor_add(
                fin[0:1, 10:11], fin[0:1, 0:1], fin[0:1, 1:2]), deps=[tf])
            t_d = b.op("vec", lambda e: e.tensor_sub(
                fin[0:1, 11:12], fin[0:1, 10:11], fin[0:1, 9:10]),
                deps=[t_b, t_c])
            t_e = b.op("vec", lambda e: e.tensor_mul(
                fin[0:1, 12:13], fin[0:1, 4:5], fin[0:1, 4:5]), deps=[tf])
            t_f = b.op("vec", lambda e: e.tensor_scalar(
                fin[0:1, 13:14], fin[0:1, 12:13], 1.0 / T, None, ALU.mult),
                deps=[t_e])
            t_g = b.op("vec", lambda e: e.tensor_sub(
                fin[0:1, 14:15], fin[0:1, 3:4], fin[0:1, 13:14]), deps=[t_f])
            tload = b.op("vec", lambda e: e.tensor_copy(fin[0:1, 15:16],
                                                        wcs[:]),
                         deps=[t_consts, tf])
            t_h = b.op("vec", lambda e: e.tensor_scalar(
                fin[0:1, 8:9], fin[0:1, 2:3], 1.0 / NL, None, ALU.mult),
                deps=[t_a, t_d])
            t_i = b.op("vec", lambda e: e.tensor_scalar(
                fin[0:1, 9:10], fin[0:1, 4:5], 1.0 / T, None, ALU.mult),
                deps=[t_b, t_d, t_h])
            t_jj = b.op("vec", lambda e: e.tensor_sub(
                fin[0:1, 10:11], fin[0:1, 8:9], fin[0:1, 9:10]),
                deps=[t_i, t_d])
            t_k = b.op("vec", lambda e: e.tensor_mul(
                fin[0:1, 12:13], fin[0:1, 10:11], fin[0:1, 10:11]),
                deps=[t_jj, t_e])
            t_l = b.op("vec", lambda e: e.tensor_scalar(
                fin[0:1, 13:14], fin[0:1, 15:16], float(T), None, ALU.mult),
                deps=[tload, t_f])
            t_m = b.op("vec", lambda e: e.tensor_scalar(
                fin[0:1, 8:9], fin[0:1, 13:14], NL, None, ALU.add),
                deps=[t_l, t_jj])
            t_n = b.op("vec", lambda e: e.reciprocal(fin[0:1, 9:10],
                                                     fin[0:1, 8:9]),
                       deps=[t_m, t_jj])
            t_o = b.op("vec", lambda e: e.tensor_mul(
                fin[0:1, 8:9], fin[0:1, 13:14], fin[0:1, 9:10]),
                deps=[t_n])
            t_p = b.op("vec", lambda e: e.tensor_scalar(
                fin[0:1, 8:9], fin[0:1, 8:9], NL, None, ALU.mult),
                deps=[t_o])
            t_q = b.op("vec", lambda e: e.tensor_mul(
                fin[0:1, 9:10], fin[0:1, 12:13], fin[0:1, 8:9]),
                deps=[t_p, t_k])
            t_r = b.op("vec", lambda e: e.tensor_mul(
                fin[0:1, 10:11], fin[0:1, 14:15], fin[0:1, 15:16]),
                deps=[t_g, t_jj, tload])
            t_s = b.op("vec", lambda e: e.tensor_add(
                fin[0:1, 12:13], fin[0:1, 11:12], fin[0:1, 10:11]),
                deps=[t_r, t_d, t_k])
            t_t = b.op("vec", lambda e: e.tensor_add(
                fin[0:1, 12:13], fin[0:1, 12:13], fin[0:1, 9:10]),
                deps=[t_s, t_q])
            t_u = b.op("vec", lambda e: e.tensor_mul(
                fin[0:1, 9:10], fin[0:1, 4:5], fin[0:1, 15:16]),
                deps=[t_q, tload])
            t_v = b.op("vec", lambda e: e.tensor_add(
                fin[0:1, 8:9], fin[0:1, 2:3], fin[0:1, 9:10]),
                deps=[t_u, t_p])
            t_w = b.op("vec", lambda e: e.tensor_scalar(
                fin[0:1, 10:11], fin[0:1, 13:14], NL, None, ALU.add),
                deps=[t_s, t_l])
            t_x1 = b.op("vec", lambda e: e.tensor_copy(fin[0:1, 0:1],
                                                       fin[0:1, 10:11]),
                        deps=[t_w, t_c])
            t_x2 = b.op("vec", lambda e: e.tensor_copy(fin[0:1, 1:2],
                                                       fin[0:1, 8:9]),
                        deps=[t_v, t_x1, t_c])
            t_x3 = b.op("vec", lambda e: e.tensor_copy(fin[0:1, 2:3],
                                                       fin[0:1, 12:13]),
                        deps=[t_t, t_x2])
            t_x4 = b.op("vec", lambda e: e.memset(fin[0:1, 3:4], 0.0),
                        deps=[t_x3])
            b.op("sync", lambda e: e.dma_start(out=stats_o,
                                               in_=fin[0:1, 0:4]),
                 deps=[t_x4], dma=True, ch="st")
        b.finish()
    return nc


def build_l3():
    nc = bass.Bass("TRN2", target_bir_lowering=False)
    inp = lambda n, s, dt: nc.dram_tensor(n, s, dt, kind="ExternalInput").ap()
    outp = lambda n, s, dt: nc.dram_tensor(n, s, dt, kind="ExternalOutput").ap()

    x_t = inp("x_t", [T, H], F32)
    y_i = inp("y_i", [T, H], F32)
    logh_i = inp("logh_i", [H, T], F32)
    stats_all = inp("stats_all", [1, 4 * NCORES], F32)
    lnw_row = inp("lnw_row", [1, H], F32)
    lnb_row = inp("lnb_row", [1, H], F32)
    ds_row = inp("ds_row", [1, H], F32)
    ones_row = inp("ones_row", [1, P], F32)
    identh = inp("identh", [P, P], FP16)

    out_o = outp("out_o", [T, H], F32)

    with ExitStack() as st:
        b = B(nc, st)
        ccst = st.enter_context(nc.sbuf_tensor("ccst", [P, 4], F32))
        tcst = b.op("vec", lambda e: e.memset(ccst[:, 0:1], EPS))
        h16 = st.enter_context(nc.sbuf_tensor("h16", [P, NHT * T], FP16))
        lnwr = st.enter_context(nc.sbuf_tensor("lnwr", [P, H], F32))
        lnbr = st.enter_context(nc.sbuf_tensor("lnbr", [P, H], F32))
        dsr = st.enter_context(nc.sbuf_tensor("dsr", [P, H], F32))
        rowbuf = st.enter_context(nc.sbuf_tensor("rowbuf", [1, 3 * H], F32))
        onrs = st.enter_context(nc.sbuf_tensor("onrs", [1, P], F32))
        idh = st.enter_context(nc.sbuf_tensor("idh", [P, P], FP16))
        stat = st.enter_context(nc.sbuf_tensor("stat", [1, 64], F32))
        gcol = st.enter_context(nc.sbuf_tensor("gcol", [P, 2], F32))
        psR = st.enter_context(nc.psum_tensor("psR", [P, H], F32))
        psC = st.enter_context(nc.psum_tensor("psC", [P, 2], F32))

        t1 = b.op("sync", lambda e: e.dma_start(out=rowbuf[0:1, 0:H],
                                                in_=lnw_row), dma=True,
                  ch="cst")
        t2 = b.op("sync", lambda e: e.dma_start(out=rowbuf[0:1, H:2 * H],
                                                in_=lnb_row), dma=True,
                  ch="cst")
        t3 = b.op("sync", lambda e: e.dma_start(out=rowbuf[0:1, 2 * H:3 * H],
                                                in_=ds_row), dma=True,
                  ch="cst")
        t4 = b.op("sync", lambda e: e.dma_start(out=onrs[:], in_=ones_row),
                  dma=True, ch="cst")
        t5 = b.op("sync", lambda e: e.dma_start(out=idh[:], in_=identh),
                  dma=True, ch="cst")
        t6 = b.op("sync", lambda e: e.dma_start(out=stat[0:1, 0:4 * NCORES],
                                                in_=stats_all), dma=True,
                  ch="cst")
        trep = []
        for i, dst in enumerate((lnwr, lnbr, dsr)):
            def rmm(e, ii=i):
                last = None
                for ns in range(4):
                    last = e.matmul(psR[:, ns * 512:(ns + 1) * 512], onrs[:],
                                    rowbuf[0:1, ii * H + ns * 512:
                                           ii * H + (ns + 1) * 512],
                                    start=True, stop=True)
                return last
            tm = b.op("te", rmm, deps=[t6, trep[-1] if trep
                                       else None])
            trep.append(b.op("vec", lambda e, dd=dst: e.tensor_copy(
                dd[:], psR[:]), deps=[tm]))

        ti = b.op("vec", lambda e: e.tensor_copy(stat[0:1, 32:35],
                                                 stat[0:1, 0:3]), deps=[t6])
        for j in range(1, NCORES):
            o = 4 * j
            ti1 = b.op("vec", lambda e: e.reciprocal(stat[0:1, 40:41],
                                                     stat[0:1, 32:33]),
                       deps=[ti])
            ti2 = b.op("vec", lambda e, oo=o: e.reciprocal(
                stat[0:1, 41:42], stat[0:1, oo:oo + 1]), deps=[ti1])
            ti3 = b.op("vec", lambda e: e.tensor_mul(
                stat[0:1, 42:43], stat[0:1, 33:34], stat[0:1, 40:41]),
                deps=[ti2])
            ti4 = b.op("vec", lambda e, oo=o: e.tensor_mul(
                stat[0:1, 43:44], stat[0:1, oo + 1:oo + 2],
                stat[0:1, 41:42]), deps=[ti3])
            ti5 = b.op("vec", lambda e: e.tensor_sub(
                stat[0:1, 44:45], stat[0:1, 42:43], stat[0:1, 43:44]),
                deps=[ti4])
            ti6 = b.op("vec", lambda e: e.tensor_mul(
                stat[0:1, 45:46], stat[0:1, 44:45], stat[0:1, 44:45]),
                deps=[ti5])
            ti7 = b.op("vec", lambda e, oo=o: e.tensor_add(
                stat[0:1, 46:47], stat[0:1, 32:33], stat[0:1, oo:oo + 1]),
                deps=[ti6])
            ti8 = b.op("vec", lambda e: e.reciprocal(stat[0:1, 47:48],
                                                     stat[0:1, 46:47]),
                       deps=[ti7])
            ti9 = b.op("vec", lambda e, oo=o: e.tensor_mul(
                stat[0:1, 48:49], stat[0:1, 32:33], stat[0:1, oo:oo + 1]),
                deps=[ti8])
            ti10 = b.op("vec", lambda e: e.tensor_mul(
                stat[0:1, 49:50], stat[0:1, 48:49], stat[0:1, 47:48]),
                deps=[ti9])
            ti11 = b.op("vec", lambda e: e.tensor_mul(
                stat[0:1, 50:51], stat[0:1, 45:46], stat[0:1, 49:50]),
                deps=[ti10])
            ti12 = b.op("vec", lambda e, oo=o: e.tensor_add(
                stat[0:1, 51:52], stat[0:1, 34:35],
                stat[0:1, oo + 2:oo + 3]), deps=[ti11])
            ti13 = b.op("vec", lambda e: e.tensor_add(
                stat[0:1, 34:35], stat[0:1, 51:52], stat[0:1, 50:51]),
                deps=[ti12])
            ti14 = b.op("vec", lambda e, oo=o: e.tensor_add(
                stat[0:1, 33:34], stat[0:1, 33:34],
                stat[0:1, oo + 1:oo + 2]), deps=[ti13])
            ti = b.op("vec", lambda e, oo=o: e.tensor_add(
                stat[0:1, 32:33], stat[0:1, 32:33], stat[0:1, oo:oo + 1]),
                deps=[ti14])
        tg1 = b.op("vec", lambda e: e.reciprocal(stat[0:1, 40:41],
                                                 stat[0:1, 32:33]), deps=[ti])
        tg2 = b.op("vec", lambda e: e.tensor_mul(
            stat[0:1, 41:42], stat[0:1, 33:34], stat[0:1, 40:41]),
            deps=[tg1])
        tg3 = b.op("vec", lambda e: e.tensor_scalar(
            stat[0:1, 42:43], stat[0:1, 32:33], -1.0, None, ALU.add),
            deps=[tg2])
        tg4 = b.op("vec", lambda e: e.reciprocal(stat[0:1, 43:44],
                                                 stat[0:1, 42:43]),
                   deps=[tg3])
        tg5 = b.op("vec", lambda e: e.tensor_mul(
            stat[0:1, 44:45], stat[0:1, 34:35], stat[0:1, 43:44]),
            deps=[tg4])
        tg6 = b.op("act", lambda e: e.activation(
            stat[0:1, 45:46], stat[0:1, 44:45], AF.Ln), deps=[tg5])
        tg7 = b.op("act", lambda e: e.activation(
            stat[0:1, 46:47], stat[0:1, 45:46], AF.Exp, scale=-0.5),
            deps=[tg6])
        tg8 = b.op("vec", lambda e: e.tensor_mul(
            stat[0:1, 47:48], stat[0:1, 41:42], stat[0:1, 46:47]),
            deps=[tg7])
        tg9 = b.op("vec", lambda e: e.tensor_scalar(
            stat[0:1, 48:49], stat[0:1, 47:48], -1.0, None, ALU.mult),
            deps=[tg8])
        tgA = b.op("vec", lambda e: e.tensor_copy(stat[0:1, 52:53],
                                                  stat[0:1, 46:47]),
                   deps=[tg7, tg9])
        tgB = b.op("vec", lambda e: e.tensor_copy(stat[0:1, 53:54],
                                                  stat[0:1, 48:49]),
                   deps=[tg9, tgA])
        tgc = b.op("te", lambda e: e.matmul(
            psC[:, 0:2], onrs[:], stat[0:1, 52:54], start=True, stop=True),
            deps=[tgB, t4])
        tgc2 = b.op("vec", lambda e: e.tensor_copy(gcol[:], psC[:, 0:2]),
                    deps=[tgc])

        with ExitStack() as s1:
            lhb = [s1.enter_context(
                nc.sbuf_tensor(f"lhb{i}", [P, T], F32)) for i in range(2)]
            tl = [None, None]
            texp = [None, None]
            tscl = [None, None]
            for hi in range(NHT):
                pi = hi % 2
                tl[pi] = b.op("sync", lambda e, h=hi, p=pi: e.dma_start(
                    out=lhb[p][:], in_=logh_i[h * P:(h + 1) * P, :]),
                    deps=[texp[pi]], dma=True, ch=f"lh{pi}")
                tscl[pi] = b.op("vec", lambda e, p=pi: e.tensor_scalar(
                    lhb[p][:], lhb[p][:], gcol[:, 0:1], gcol[:, 1:2],
                    ALU.mult, ALU.add), deps=[tl[pi], tgc2, tscl[pi]])
                texp[pi] = b.op("act", lambda e, h=hi, p=pi: e.activation(
                    h16[:, h * T:(h + 1) * T], lhb[p][:], AF.Exp),
                    deps=[tscl[pi]])
            th16 = [texp[0], texp[1]]

            xb = [s1.enter_context(
                nc.sbuf_tensor(f"xb{i}", [P, H], F32)) for i in range(2)]
            yb = [s1.enter_context(
                nc.sbuf_tensor(f"yb{i}", [P, H], F32)) for i in range(2)]
            hA = s1.enter_context(nc.sbuf_tensor("hA", [P, H], FP16))
            xnb = s1.enter_context(nc.sbuf_tensor("xnb", [P, H], F32))
            sb_ = s1.enter_context(nc.sbuf_tensor("sb_", [P, H], F32))
            sqd = s1.enter_context(nc.sbuf_tensor("sqd", [P, H], F32))
            prb = s1.enter_context(nc.sbuf_tensor("prb", [P, H], F32))
            ob = s1.enter_context(nc.sbuf_tensor("ob", [P, H], F32))
            cl2 = s1.enter_context(nc.sbuf_tensor("cl2", [P, 8], F32))
            psT = [s1.enter_context(
                nc.psum_tensor(f"psT{i}", [P, 1024], FP16))
                for i in range(2)]

            tx = [None, None]
            ty = [None, None]
            tdone = None
            tev = {0: None, 1: None}
            tuse = None
            act_prev = None
            for tcn in range(NTT):
                pi = tcn % 2
                tx[pi] = b.op("sync", lambda e, t=tcn, p=pi: e.dma_start(
                    out=xb[p][:], in_=x_t[t * P:(t + 1) * P, :]),
                    deps=[tdone], dma=True, ch=f"x{pi}")
                ty[pi] = b.op("sync", lambda e, t=tcn, p=pi: e.dma_start(
                    out=yb[p][:], in_=y_i[t * P:(t + 1) * P, :]),
                    deps=[tdone], dma=True, ch=f"y{pi}")
                for q in range(4):
                    qp = q % 2
                    def tpg(e, qq=q, t=tcn, qpp=qp):
                        last = None
                        for j in range(4):
                            hi = qq * 4 + j
                            last = e.transpose(
                                psT[qpp][:, j * P:(j + 1) * P],
                                h16[:, hi * T + t * P:hi * T + (t + 1) * P],
                                idh[:])
                        return last
                    ttp = b.op("te", tpg, deps=[th16, t5, tev[qp]])
                    tev[qp] = b.op("vec", lambda e, qq=q, qpp=qp:
                                   e.tensor_copy(
                        hA[:, qq * 512:(qq + 1) * 512],
                        psT[qpp][:, 0:512]),
                        deps=[ttp, tuse])
                t1_ = b.op("vec", lambda e, p=pi: e.reduce_sum(
                    cl2[:, 0:1], xb[p][:], AX.X, negate=True),
                    deps=[tx[pi], tdone])
                t2_ = b.op("vec", lambda e: e.tensor_scalar(
                    cl2[:, 1:2], cl2[:, 0:1], 1.0 / H, None, ALU.mult),
                    deps=[t1_])
                t3_ = b.op("vec", lambda e, p=pi: e.tensor_scalar(
                    xnb[:], xb[p][:], cl2[:, 1:2], None, ALU.add),
                    deps=[t2_, act_prev])
                t4_ = b.op("act", lambda e: e.activation(
                    sqd[:], xnb[:], AF.Square, accum_out=cl2[:, 2:3]),
                    deps=[t3_])
                t5_ = b.op("vec", lambda e: e.tensor_scalar(
                    cl2[:, 3:4], cl2[:, 2:3], 1.0 / H, None, ALU.mult),
                    deps=[t4_])
                t6_ = b.op("act", lambda e: e.activation(
                    cl2[:, 4:5], cl2[:, 3:4], AF.Ln, bias=ccst[:, 0:1]), deps=[t5_])
                t7_ = b.op("act", lambda e: e.activation(
                    cl2[:, 5:6], cl2[:, 4:5], AF.Exp, scale=-0.5), deps=[t6_])
                t8_ = b.op("vec", lambda e: e.tensor_scalar(
                    xnb[:], xnb[:], cl2[:, 5:6], None, ALU.mult),
                    deps=[t7_])
                t9_ = b.op("vec", lambda e: e.tensor_add(sb_[:], hA[:],
                                                         xnb[:]),
                           deps=[t8_, tev[0], tev[1]])
                tuse = t9_
                t10 = b.op("vec", lambda e: e.reduce_sum(
                    cl2[:, 0:1], sb_[:], AX.X, negate=True), deps=[t9_])
                t11 = b.op("vec", lambda e: e.tensor_scalar(
                    cl2[:, 1:2], cl2[:, 0:1], 1.0 / H, None, ALU.mult),
                    deps=[t10])
                t12 = b.op("act", lambda e: e.activation(
                    sqd[:], sb_[:], AF.Square, accum_out=cl2[:, 2:3]),
                    deps=[t9_, t4_])
                t13 = b.op("vec", lambda e: e.tensor_scalar(
                    cl2[:, 3:4], cl2[:, 2:3], 1.0 / H, None, ALU.mult),
                    deps=[t12])
                t14 = b.op("vec", lambda e: e.tensor_mul(
                    cl2[:, 6:7], cl2[:, 1:2], cl2[:, 1:2]), deps=[t11, t13])
                t15 = b.op("vec", lambda e: e.tensor_sub(
                    cl2[:, 4:5], cl2[:, 3:4], cl2[:, 6:7]), deps=[t14])
                t16 = b.op("act", lambda e: e.activation(
                    cl2[:, 5:6], cl2[:, 4:5], AF.Ln, bias=ccst[:, 0:1]), deps=[t15])
                t17 = b.op("act", lambda e: e.activation(
                    cl2[:, 6:7], cl2[:, 5:6], AF.Exp, scale=-0.5),
                    deps=[t16])
                act_prev = t17
                t18 = b.op("vec", lambda e: e.tensor_scalar(
                    prb[:], sb_[:], cl2[:, 1:2], cl2[:, 6:7], ALU.add,
                    ALU.mult), deps=[t17, t9_])
                t19 = b.op("vec", lambda e: e.tensor_mul(ob[:], prb[:],
                                                         lnwr[:]),
                           deps=[t18, trep[0], tdone])
                t20 = b.op("vec", lambda e: e.tensor_add(ob[:], ob[:],
                                                         lnbr[:]),
                           deps=[t19, trep[1]])
                t21 = b.op("vec", lambda e, p=pi: e.scalar_tensor_tensor(
                    prb[:], yb[p][:], 1.0, dsr[:], ALU.mult, ALU.mult),
                    deps=[ty[pi], trep[2], t18])
                t22 = b.op("vec", lambda e: e.tensor_add(ob[:], ob[:],
                                                         prb[:]),
                           deps=[t20, t21])
                t23 = b.op("vec", lambda e, p=pi: e.tensor_add(
                    ob[:], ob[:], xb[p][:]), deps=[t22])
                tdone = b.op("sync", lambda e, t=tcn: e.dma_start(
                    out=out_o[t * P:(t + 1) * P, :], in_=ob[:]),
                    deps=[t23], dma=True, ch="out")
        b.finish()
    return nc


def _prep_common(inputs):
    Wz = np.ascontiguousarray(inputs["L0_attn_W_z"])
    Wh = np.ascontiguousarray(inputs["L0_attn_W_h"])
    Wg = np.ascontiguousarray(inputs["L0_ffn_gate_w"])
    Wu = np.ascontiguousarray(inputs["L0_ffn_up_w"])
    Wd = np.ascontiguousarray(inputs["L0_ffn_down_w"])

    def blk_hh(W):
        t4 = np.ascontiguousarray(W.T).reshape(NHT, P, NHT, P)
        return np.ascontiguousarray(
            t4.transpose(2, 1, 0, 3).reshape(NHT, P, NHT * P))

    def blk_mh(W):
        t4 = np.ascontiguousarray(W.T).reshape(NHT, P, MT, P)
        return np.ascontiguousarray(
            t4.transpose(2, 1, 0, 3).reshape(MT, P, NHT * P))

    d = {}
    d["wz_blk"] = blk_hh(Wz)
    d["wh_blk"] = blk_hh(Wh)
    d["wg_blk"] = blk_mh(Wg)
    d["wu_blk"] = blk_mh(Wu)
    d["wd_t"] = np.ascontiguousarray(Wd.T)
    d["bz_c"] = np.ascontiguousarray(
        inputs["L0_attn_b_z"].reshape(NHT, P).T)
    d["bh_c"] = np.ascontiguousarray(
        inputs["L0_attn_b_h"].reshape(NHT, P).T)
    d["sg_c"] = np.ascontiguousarray(
        inputs["L0_ffn_gate_scale"].reshape(MT, P).T)
    d["su_c"] = np.ascontiguousarray(
        inputs["L0_ffn_up_scale"].reshape(MT, P).T)
    d["tri"] = np.ascontiguousarray(np.triu(np.ones((P, P), np.float32)))
    d["ones_row"] = np.ones((1, P), np.float32)
    d["ones_col"] = np.ones((P, 1), np.float32)
    d["identb"] = np.eye(P, dtype=ml_dtypes.bfloat16)
    d["identh"] = np.eye(P, dtype=np.float16)
    d["h0_pm"] = np.ascontiguousarray(
        inputs["L0_attn_h0"].reshape(P, NHT)).astype(np.float32)
    d["lnw_row"] = np.ascontiguousarray(inputs["L0_attn_ln_w"][None, :])
    d["lnb_row"] = np.ascontiguousarray(inputs["L0_attn_ln_b"][None, :])
    d["ds_row"] = np.ascontiguousarray(inputs["L0_ffn_down_scale"][None, :])
    return d


def kernel(**inputs):
    inputs = {k: np.asarray(v) for k, v in inputs.items()}
    x = inputs["L0_attn_hidden_states"]
    com = _prep_common(inputs)

    x_slices = []
    for c in range(NCORES):
        bsel = c // 4
        toff = (c % 4) * T
        x_slices.append(np.ascontiguousarray(x[bsel, toff:toff + T, :]))

    mask_list = []
    for c in range(NCORES):
        m = np.full((P, 144), NEG_BIG, np.float32)
        m[:, 128:144] = 0.0
        grp0 = (c // 4) * 4
        for j in range(grp0, c):
            m[:, (j * NHT):((j + 1) * NHT)] = 0.0
        mask_list.append(m)

    nc1 = build_l1()
    maps1 = []
    for c in range(NCORES):
        mp = {"x_t": x_slices[c]}
        for k in ("wz_blk", "wh_blk", "bz_c", "bh_c", "wg_blk", "wu_blk",
                  "wd_t", "sg_c", "su_c", "tri", "ones_row", "ones_col",
                  "identb"):
            mp[k] = com[k]
        maps1.append(mp)
    r1 = run_bass_kernel_spmd(nc1, maps1, core_ids=list(range(NCORES)))

    G_pm = np.zeros((NCORES, P, NHT), np.float32)
    for c in range(NCORES):
        Lcol = np.asarray(r1.results[c]["L_o"])
        vec = np.ascontiguousarray(Lcol.T).reshape(H)
        G_pm[c] = vec.reshape(P, NHT)

    nc2 = build_l2()
    maps2 = []
    for c in range(NCORES):
        mp = {
            "xnT_i": np.asarray(r1.results[c]["xnT_o"]),
            "w16_i": np.asarray(r1.results[c]["w16_o"]),
            "G_pm": G_pm,
            "h0_pm": com["h0_pm"],
            "masks": mask_list[c],
            "wz_blk": com["wz_blk"],
            "bz_c": com["bz_c"],
            "tri": com["tri"],
            "ones_row": com["ones_row"],
            "ones_col": com["ones_col"],
            "wc_in": np.array([[1.0 if c % 4 == 0 else 0.0]], np.float32),
        }
        maps2.append(mp)
    r2 = run_bass_kernel_spmd(nc2, maps2, core_ids=list(range(NCORES)))

    stats_all = np.concatenate(
        [np.asarray(r2.results[c]["stats_o"]) for c in range(NCORES)],
        axis=1).astype(np.float32)

    nc3 = build_l3()
    maps3 = []
    for c in range(NCORES):
        mp = {
            "x_t": x_slices[c],
            "y_i": np.asarray(r1.results[c]["y_o"]),
            "logh_i": np.asarray(r2.results[c]["logh_o"]),
            "stats_all": stats_all,
            "lnw_row": com["lnw_row"],
            "lnb_row": com["lnb_row"],
            "ds_row": com["ds_row"],
            "ones_row": com["ones_row"],
            "identh": com["identh"],
        }
        maps3.append(mp)
    r3 = run_bass_kernel_spmd(nc3, maps3, core_ids=list(range(NCORES)))

    out = np.zeros((2, 8192, H), np.float32)
    for c in range(NCORES):
        bsel = c // 4
        toff = (c % 4) * T
        out[bsel, toff:toff + T, :] = np.asarray(r3.results[c]["out_o"])
    return out


if __name__ == "__main__":
    rng = np.random.RandomState(0)
    print("smoke build only")
    nc = build_l2()
    print("l2 built")
